# version 04
# baseline (speedup 1.0000x reference)
import numpy as np
import ml_dtypes
import concourse.bass as bass
import concourse.mybir as mybir
from concourse.bass_utils import run_bass_kernel_spmd

F32 = mybir.dt.float32
BF16 = mybir.dt.bfloat16
I32 = mybir.dt.int32
U8 = mybir.dt.uint8
AF = mybir.ActivationFunctionType
ALU = mybir.AluOpType

S, D, H, DH, HID = 4096, 1024, 16, 64, 3072
NPAIR = 8
QBA = 1024
NQBA = S // QBA
QBF = 256
NBF = S // QBF
S127 = 1.0 / 127.0
SM_SCALE = 0.125
CBIAS = -3.0
MASKV = -960.0
EPS = 1e-5
ARENA = 210944

O_XNT = 0
O_QT = 65536
O_KT = 81920
O_V = 98304
O_WSL = 114944
O_EXPS = 127232
O_BCB = 131328
O_RCPZ = 135424
O_IDN = 139520
O_CB = 139776
O_REC = 139840
O_TRIM = 140032
O_EPSC = 140288
O_X1 = 141312
O_XN1 = 149504
O_JUNK = 153600
O_COL1 = 155648
O_WGT = 0
O_X3 = 141312
O_AT3 = 149504
O_XN2 = 157696
O_ZLD = 161792
O_COL3 = 165888
O_JUNK3 = 166912
O_WOT = 192512
O_WUT = 49152
O_WDT = 98304
O_X2T4 = 147456
O_ACTT = 155648
O_SILU = 167936
O_RES4 = 169984
O_OUT4 = 178176


class Sched:

    def __init__(self, nc):
        self.nc = nc
        self.ops = {e: [] for e in ("sync", "scalar", "vector", "tensor", "gpsimd")}
        self.sems = {}
        self.cnt = {}
        self._stack = []

    def sem(self, name):
        g = self.nc.semaphore(name)
        h = g.__enter__()
        self._stack.append(g)
        self.sems[name] = h
        self.cnt[name] = 0

    def wait(self, eng, name, val=None):
        v = self.cnt[name] if val is None else val
        if v <= 0:
            return
        h = self.sems[name]
        self.ops[eng].append(lambda e, h=h, v=v: e.wait_ge(h, v))

    def op(self, eng, fn, inc=None, k=1):
        if inc is None:
            self.ops[eng].append(fn)
        else:
            h = self.sems[inc]
            self.cnt[inc] += k
            self.ops[eng].append(lambda e, fn=fn, h=h, k=k: fn(e).then_inc(h, k))

    def c(self, name):
        return self.cnt[name]


def build(dbg=False, ph=4):
    nc = bass.Bass("TRN2", target_bir_lowering=False)
    x_d = nc.dram_tensor("x", [S, D], F32, kind="ExternalInput").ap()
    wq_d = nc.dram_tensor("wq", [D, D], F32, kind="ExternalInput").ap()
    wk_d = nc.dram_tensor("wk", [D, D], F32, kind="ExternalInput").ap()
    wv_d = nc.dram_tensor("wv", [D, D], F32, kind="ExternalInput").ap()
    wo_d = nc.dram_tensor("wo", [D, D], F32, kind="ExternalInput").ap()
    wg_d = nc.dram_tensor("wg", [HID, D], I32, kind="ExternalInput").ap()
    wu_d = nc.dram_tensor("wu", [HID, D], I32, kind="ExternalInput").ap()
    wd_d = nc.dram_tensor("wd", [D, HID], I32, kind="ExternalInput").ap()
    tm_d = nc.dram_tensor("trimask", [128, 128], BF16, kind="ExternalInput").ap()
    id_d = nc.dram_tensor("idn", [128, 128], BF16, kind="ExternalInput").ap()
    out_d = nc.dram_tensor("out", [S, D], F32, kind="ExternalOutput").ap()
    ik = "ExternalOutput" if dbg else "Internal"
    xn_d = nc.dram_tensor("xn_bf", [S, D], BF16, kind=ik).ap()
    af_d = nc.dram_tensor("attn_f", [S, D], F32, kind=ik).ap()
    x2_d = nc.dram_tensor("xn2_bf", [S, D], BF16, kind=ik).ap()
    zt_d = nc.dram_tensor("zt_bf", [D, S], BF16, kind="Internal").ap()
    wqb_d = nc.dram_tensor("wq_bf", [D, D], BF16, kind="Internal").ap()
    wkb_d = nc.dram_tensor("wk_bf", [D, D], BF16, kind="Internal").ap()
    wvb_d = nc.dram_tensor("wv_bf", [D, D], BF16, kind="Internal").ap()
    wob_d = nc.dram_tensor("wo_bf", [D, D], BF16, kind="Internal").ap()
    wgb_d = nc.dram_tensor("wg_bf", [HID, D], BF16, kind="Internal").ap()
    wub_d = nc.dram_tensor("wu_bf", [HID, D], BF16, kind="Internal").ap()
    wdb_d = nc.dram_tensor("wd_bf", [D, HID], BF16, kind="Internal").ap()
    rc_d = nc.dram_tensor("rc", [H, S], F32, kind="Internal").ap()
    rc2_d = nc.dram_tensor("rc2", [H, S], F32, kind="Internal").ap()

    with (
        nc.sbuf_tensor("arena", [128, ARENA], U8) as arena,
        nc.psum_tensor("ps0", [128, 1024], F32) as ps0,
        nc.psum_tensor("ps1", [128, 1024], F32) as ps1,
        nc.psum_tensor("ps2", [128, 1024], F32) as ps2,
        nc.psum_tensor("ps3", [128, 1024], F32) as ps3,
    ):
        PS = [ps0, ps1, ps2, ps3]

        def cv(off, p, n, dt, p0=0):
            nb = {BF16: 2, F32: 4, I32: 4}[dt] * n
            return arena[p0:p0 + p, off:off + nb].bitcast(dt)

        xnT = cv(O_XNT, 128, 32768, BF16)
        qt = [cv(O_QT + par * 8192, 128, 4096, BF16) for par in (0, 1)]
        kt = [cv(O_KT + par * 8192, 128, 4096, BF16) for par in (0, 1)]
        vsb = [cv(O_V + par * 8320, 128, 4160, BF16) for par in (0, 1)]
        wsl = [[cv(O_WSL + (m * 2 + par) * 2048, 128, 1024, BF16) for par in (0, 1)]
               for m in range(3)]
        exps = [cv(O_EXPS, 128, 1024, BF16), cv(O_EXPS + 2048, 128, 1024, BF16),
                cv(174080, 128, 1024, BF16), cv(176128, 128, 1024, BF16)]
        zstg = [cv(O_RCPZ + par * 2048, 64, 1024, BF16, p0=64) for par in (0, 1)]
        zun = [cv(141312 + hp * 8192, 64, 4096, BF16) for hp in (0, 1)]
        bcbH = [cv(157696 + hp * 20480, 64, 4096, F32) for hp in (0, 1)]
        rcp4 = [cv(194560, 1, 4096, F32, p0=32 * hp) for hp in (0, 1)]
        recH_t = [cv(140352 + hp * 128, 128, 32, F32) for hp in (0, 1)]
        recH_r = [cv(140608 + hp * 128, 128, 32, F32) for hp in (0, 1)]
        idn = cv(O_IDN, 128, 128, BF16)
        cbias = cv(O_CB, 128, 1, F32)
        trim = cv(O_TRIM, 128, 128, BF16)
        eps_c = cv(O_EPSC, 128, 1, F32)
        x1 = [cv(O_X1 + par * 4096, 128, 1024, F32) for par in (0, 1)]
        xn1 = [cv(O_XN1 + par * 2048, 128, 1024, BF16) for par in (0, 1)]
        junk = cv(O_JUNK, 128, 1024, BF16)
        wgT = cv(O_WGT, 128, 24576, BF16)
        x3 = [cv(O_X3 + par * 4096, 128, 1024, F32) for par in (0, 1)]
        at3 = [cv(O_AT3 + par * 4096, 128, 1024, F32) for par in (0, 1)]
        xn2 = [cv(O_XN2 + par * 2048, 128, 1024, BF16) for par in (0, 1)]
        zld = [cv(O_ZLD + par * 2048, 128, 1024, BF16) for par in (0, 1)]
        junk3 = cv(O_JUNK3, 128, 1024, BF16)
        wot = cv(O_WOT, 128, 8192, BF16)
        wuT = cv(O_WUT, 128, 24576, BF16)
        wdT = cv(O_WDT, 128, 24576, BF16)
        x2t4 = [cv(O_X2T4 + par * 4096, 128, 2048, BF16) for par in (0, 1)]
        actT = cv(O_ACTT, 128, 6144, BF16)
        silu = [cv(O_SILU + par * 1024, 128, 256, F32) for par in (0, 1)]
        res4 = [cv(O_RES4 + par * 4096, 128, 1024, F32) for par in (0, 1)]
        out4 = [cv(O_OUT4 + par * 4096, 128, 1024, F32) for par in (0, 1)]

        def col(base, idx):
            return cv(base + idx * 64, 128, 1, F32)

        sch = Sched(nc)
        for s in ("vch", "sln", "pre", "cast", "xld", "ssum", "vstat", "sistd",
                  "vaff", "xnst", "xnt", "wsl", "tq", "vq", "ts", "es", "ta",
                  "vr", "rcs", "bc1", "vr2", "rcs2", "bcs", "vz", "zsp", "vmul", "wgt",
                  "wot", "wut", "wdt2", "tw", "x2", "va", "zl", "s2sum", "v2stat",
                  "s2istd", "v2aff", "asp", "x2st", "x2t", "tg", "tu", "ss",
                  "vact", "td", "rl", "vo", "ost"):
            sch.sem(s)

        sch.op("sync", lambda e: e.dma_start(trim, tm_d), inc="pre", k=16)
        sch.op("sync", lambda e: e.dma_start(idn, id_d), inc="pre", k=16)
        sch.op("vector", lambda e: e.memset(cbias, CBIAS), inc="pre", k=1)
        sch.op("vector", lambda e: e.memset(vsb[0][:, 64::65], 1.0), inc="pre", k=1)
        sch.op("vector", lambda e: e.memset(vsb[1][:, 64::65], 1.0), inc="pre", k=1)
        sch.op("vector", lambda e: e.memset(eps_c, EPS), inc="pre", k=1)
        PRE_N = 36
        sch.wait("vector", "pre", PRE_N)
        for w_src, w_dst in ((wq_d, wqb_d), (wk_d, wkb_d), (wv_d, wvb_d),
                             (wo_d, wob_d), (wg_d, wgb_d), (wu_d, wub_d),
                             (wd_d, wdb_d)):
            sch.op("gpsimd", lambda e, a=w_dst, b=w_src: e.dma_start(a, b),
                   inc="cast", k=16)
        CAST = {"q": 16, "k": 32, "v": 48, "o": 64, "g": 80, "u": 96, "d": 112}

        def ln_tile(i, src, dst, cbase, jk, sems, in_ready, war_sync, prev):
            s_sum, s_vst, s_ist, s_aff = sems
            p = i % 2
            sumc, sqc = col(cbase, 0 + p), col(cbase, 2 + p)
            mc, msqc = col(cbase, 4 + p), col(cbase, 6 + p)
            varc, t1c = col(cbase, 8 + p), col(cbase, 10 + p)
            istdc, negmc = col(cbase, 12 + p), col(cbase, 14 + p)
            rec = {}

            def sums():
                in_ready("scalar")
                if prev:
                    sch.wait("scalar", "vch", prev["mc"])
                    sch.wait("scalar", s_vst, i - 1)
                sch.op("scalar", lambda e: e.activation(jk, src, AF.Copy,
                                                        accum_out=sumc),
                       inc=s_sum, k=1)
                sch.op("scalar", lambda e: e.activation(jk, src, AF.Square,
                                                        accum_out=sqc),
                       inc=s_sum, k=1)

            def stats():
                sch.wait("vector", s_sum, 2 * (i + 1))
                if prev:
                    sch.wait("vector", "vch", prev["negm"])
                    sch.wait("vector", s_vst, i - 1)
                    sch.wait("vector", "sln", prev["ln"])
                sch.op("vector", lambda e: e.tensor_scalar_mul(mc, sumc, 1.0 / D),
                       inc="vch", k=1)
                rec["mc"] = sch.c("vch")
                sch.wait("vector", "vch", rec["mc"])
                sch.op("vector", lambda e: e.tensor_mul(msqc, mc, mc),
                       inc="vch", k=1)
                sch.wait("vector", "vch", sch.c("vch"))
                sch.op("vector", lambda e: e.scalar_tensor_tensor(
                    varc, sqc, 1.0 / D, msqc, ALU.mult, ALU.subtract),
                    inc=s_vst, k=1)

            def lnexp():
                sch.wait("scalar", s_vst, i + 1)
                if prev:
                    sch.wait("scalar", s_ist, i - 1)
                    sch.wait("scalar", s_aff, i - 1)
                sch.op("scalar", lambda e: e.activation(t1c, varc, AF.Ln,
                                                        bias=eps_c),
                       inc="sln", k=1)
                rec["ln"] = sch.c("sln")
                sch.wait("scalar", "sln", rec["ln"])
                sch.op("scalar", lambda e: e.activation(istdc, t1c, AF.Exp,
                                                        scale=-0.5),
                       inc=s_ist, k=1)

            def affine():
                sch.wait("vector", s_ist, i + 1)
                if prev:
                    sch.wait("vector", s_aff, i - 1)
                sch.op("vector", lambda e: e.scalar_tensor_tensor(
                    negmc, mc, -1.0, istdc, ALU.mult, ALU.mult),
                    inc="vch", k=1)
                rec["negm"] = sch.c("vch")
                war_sync("vector")
                sch.wait("vector", "vch", rec["negm"])
                sch.op("vector", lambda e: e.tensor_scalar(
                    dst, src, istdc, negmc, ALU.mult, ALU.add),
                    inc=s_aff, k=1)

            return sums, stats, lnexp, affine, rec

        p1 = []
        for i in range(32):
            src, dst = x1[i % 2], xn1[i % 2]

            def in_ready(eng, i=i):
                sch.wait(eng, "xld", 16 * (i + 1))

            def war(eng, i=i):
                sch.wait(eng, "xnst", 16 * max(0, i - 1))

            p1.append(ln_tile(i, src, dst, O_COL1, junk,
                              ("ssum", "vstat", "sistd", "vaff"), in_ready, war,
                              p1[i - 2][4] if i >= 2 else None))

        sch.wait("scalar", "pre", PRE_N)
        for i in range(32):
            sch.wait("sync", "ssum", 2 * max(0, i - 1))
            sch.wait("sync", "vaff", max(0, i - 1))
            sch.op("sync", lambda e, i=i: e.dma_start(
                x1[i % 2], x_d[128 * i:128 * i + 128, :]), inc="xld", k=16)
            p1[i][0]()
            p1[i][1]()
            if i >= 1:
                p1[i - 1][2]()
                p1[i - 1][3]()
                sch.wait("sync", "vaff", i)
                sch.op("sync", lambda e, j=i - 1: e.dma_start(
                    xn_d[128 * j:128 * j + 128, :], xn1[j % 2]), inc="xnst", k=16)
        p1[31][2]()
        p1[31][3]()
        sch.wait("sync", "vaff", 32)
        sch.op("sync", lambda e: e.dma_start(xn_d[128 * 31:128 * 32, :], xn1[1]),
               inc="xnst", k=16)
        sch.wait("sync", "xnst", 16 * 32)
        for dc in range(8):
            sch.op("sync", lambda e, dc=dc: e.dma_start(
                xnT[:, dc * 4096:(dc + 1) * 4096], xn_d[:, dc * 128:(dc + 1) * 128],
                transpose=True), inc="xnt", k=16)

        def wslice_loads(pair):
            par = pair % 2
            sch.wait("sync", "tq", 48 * max(0, pair - 1))
            sch.wait("sync", "cast", CAST["v"])
            for m, wb in enumerate((wqb_d, wkb_d, wvb_d)):
                for dc in range(8):
                    sch.op("sync", lambda e, m=m, par=par, dc=dc, wb=wb, pr=pair:
                           e.dma_start(wsl[m][par][:, dc * 128:(dc + 1) * 128],
                                       wb[pr * 128:(pr + 1) * 128,
                                          dc * 128:(dc + 1) * 128],
                                       transpose=True), inc="wsl", k=16)

        state = {"u": 0, "k": 0, "mi": 0}
        es_hist = []
        ta_hist = []
        vz_hist = [0, 0]
        zsp_hist = [0, 0]
        rcs_hist = [0, 0]
        vr2_hist = [0, 0]
        rcs2_hist = [0, 0]
        vmul_hist = [0, 0]
        vq_by_pair = {}

        borrowed = {0: 0, 1: 0}

        def qkv_group_maker(pair):
            par = pair % 2
            gstate = {"gi": 0, "half": 0}
            vq_hist = []
            vq_by_pair[pair] = vq_hist

            def emit_group(wt, psum_wait=None):
                gi = gstate["gi"]
                ps = PS[2 + wt]
                half = gstate["half"]
                gstate["half"] ^= 1
                psl = slice(0, 512) if half == 0 else slice(512, 1024)
                if gi >= 2:
                    sch.wait("tensor", "vq", vq_hist[gi - 2])
                if psum_wait:
                    psum_wait()
                if pair == 0 and gi == 0:
                    sch.wait("tensor", "pre", PRE_N)
                    sch.wait("tensor", "xnt", 16 * 8)
                sch.wait("tensor", "wsl", 16 * 24 * (pair + 1))
                if gi < 8:
                    qb = gi
                    lhsT = lambda dc, par=par: wsl[0][par][:, dc * 128:(dc + 1) * 128]
                    rhs = lambda dc, qb=qb: xnT[:, dc * 4096 + qb * 512:
                                                dc * 4096 + (qb + 1) * 512]
                    n = 512
                elif gi < 16:
                    qb = gi - 8
                    lhsT = lambda dc, par=par: wsl[1][par][:, dc * 128:(dc + 1) * 128]
                    rhs = lambda dc, qb=qb: xnT[:, dc * 4096 + qb * 512:
                                                dc * 4096 + (qb + 1) * 512]
                    n = 512
                else:
                    pt = gi - 16
                    lhsT = lambda dc, pt=pt: xnT[:, dc * 4096 + pt * 128:
                                                 dc * 4096 + (pt + 1) * 128]
                    rhs = lambda dc, par=par: wsl[2][par][:, dc * 128:(dc + 1) * 128]
                    n = 128
                for dc in range(8):
                    sch.op("tensor", lambda e, dc=dc, ps=ps, l=lhsT, r=rhs, n=n,
                           psl=psl: e.matmul(ps[0:128, psl][:, 0:n], l(dc),
                                             r(dc), start=(dc == 0),
                                             stop=(dc == 7)),
                           inc="tq" if dc == 7 else None, k=1)
                borrowed[wt] = None
                sch.wait("vector", "tq", sch.c("tq"))
                if gi < 8:
                    qb = gi
                    sch.op("vector", lambda e, ps=ps, qb=qb, par=par, psl=psl:
                           e.tensor_copy(qt[par][:, qb * 512:(qb + 1) * 512],
                                         ps[0:128, psl]), inc="vq", k=1)
                elif gi < 16:
                    qb = gi - 8
                    sch.op("vector", lambda e, ps=ps, qb=qb, par=par, psl=psl:
                           e.tensor_copy(kt[par][:, qb * 512:(qb + 1) * 512],
                                         ps[0:128, psl]), inc="vq", k=1)
                else:
                    pt = gi - 16
                    sch.op("vector", lambda e, ps=ps, pt=pt, par=par, psl=psl:
                           e.tensor_copy(vsb[par][:, pt * 65:pt * 65 + 64],
                                         ps[0:128, psl][:, 0:64]), inc="vq", k=1)
                    sch.op("vector", lambda e, ps=ps, pt=pt, par=par, psl=psl:
                           e.tensor_copy(
                               vsb[par][:, 2080 + pt * 65:2080 + pt * 65 + 64],
                               ps[0:128, psl][:, 64:128]), inc="vq", k=1)
                vq_hist.append(sch.c("vq"))
                borrowed[wt] = sch.c("vq")
                gstate["gi"] += 1

            return emit_group

        def attention(pair):
            par = pair % 2
            nxt = qkv_group_maker(pair + 1) if pair + 1 < NPAIR else None
            n_iters = 160
            it = [0]
            gquota = [0]

            def maybe_groups():
                if nxt is None:
                    return
                it[0] += 1
                want = max(0, (it[0] - 33) * 48 // (n_iters - 33))
                while gquota[0] < min(48, want):
                    u = state["u"]
                    wt = (u + 1) % 2

                    def psum_wait(u=u):
                        sch.wait("tensor", "vz", vz_hist[u + 1])
                    nxt(wt, psum_wait)
                    gquota[0] += 1

            for hh in (0, 1):
                hg = 2 * pair + hh
                hp = hg % 2
                fb = 64 * hh
                vo_ = 2080 * hh
                for qb in range(NQBA):
                    q0 = qb * QBA
                    u = state["u"]
                    zq = u % 2
                    psz = PS[2 + zq]
                    npt = 8 * qb + 8
                    pend = []

                    def emit_A(psz=psz, vo_=vo_, npt=npt, u=u, par=par):
                        pt, segs, sp, es_at = pend.pop(0)
                        sch.wait("tensor", "es", es_at)
                        if pt == 0:
                            sch.wait("tensor", "vz", vz_hist[u])
                            sch.wait("tensor", "vq", borrowed[u % 2])
                        for si, (a, b) in enumerate(segs):
                            sch.op("tensor", lambda e, a=a, b=b, psz=psz, vo_=vo_,
                                   pt=pt, npt=npt, sp=sp, par=par: e.matmul(
                                       psz[0:65, a:b],
                                       vsb[par][:, vo_ + pt * 65:vo_ + pt * 65 + 65],
                                       exps[sp][:, a:b],
                                       start=(pt == 0), stop=(pt == npt - 1)),
                                   inc="ta" if si == len(segs) - 1 else None, k=1)
                        ta_hist.append(sch.c("ta"))
                        maybe_groups()

                    for pt in range(npt):
                        k = state["k"]
                        sp = k % 4
                        pss = PS[k % 2]
                        p0 = 128 * pt
                        co = max(0, p0 - q0)
                        segs = [(co, 512), (512, 1024)] if co < 512 else [(co, 1024)]
                        diag = pt >= 8 * qb
                        if k >= 2:
                            sch.wait("tensor", "es", es_hist[k - 2])
                        if pt == 0 and hh == 0 and qb == 0:
                            sch.wait("tensor", "vq", sch.c("vq"))
                        for si, (a, b) in enumerate(segs):
                            first = si == 0
                            last_seg = si == len(segs) - 1
                            dm = diag and a == co
                            sch.op("tensor", lambda e, a=a, b=b, pss=pss, fb=fb,
                                   p0=p0, q0=q0, par=par, dm=dm: e.matmul(
                                       pss[:, a:b],
                                       kt[par][fb:fb + 64, p0:p0 + 128],
                                       qt[par][fb:fb + 64, q0 + a:q0 + b],
                                       start=True, stop=not dm),
                                   inc=None if dm else
                                       ("ts" if last_seg else None), k=1)
                            if dm:
                                sch.op("tensor", lambda e, a=a, pss=pss: e.matmul(
                                    pss[:, a:a + 128], idn, trim,
                                    start=False, stop=True),
                                    inc="ts" if last_seg else None, k=1)
                        sch.wait("scalar", "ts", sch.c("ts"))
                        if k >= 4:
                            sch.wait("scalar", "ta", ta_hist[k - 4])
                        sch.op("scalar", lambda e, sp=sp, co=co, pss=pss:
                               e.activation(exps[sp][:, co:1024], pss[:, co:1024],
                                            AF.Exp, bias=cbias, scale=SM_SCALE),
                               inc="es", k=1)
                        es_hist.append(sch.c("es"))
                        pend.append((pt, segs, sp, sch.c("es")))
                        if len(pend) > 2:
                            emit_A()
                        state["k"] += 1
                    while pend:
                        emit_A()
                    sch.wait("vector", "ta", sch.c("ta"))
                    sch.wait("vector", "vmul", vmul_hist[hg])
                    sch.op("vector", lambda e, hp=hp, q0=q0, psz=psz: e.tensor_copy(
                        zun[hp][:, q0:q0 + QBA], psz[0:64, 0:1024]), inc="vz", k=1)
                    sch.wait("vector", "rcs", rcs_hist[hg])
                    sch.op("vector", lambda e, hp=hp, q0=q0, psz=psz: e.tensor_copy(
                        rcp4[hp][:, q0:q0 + QBA], psz[64:65, 0:1024]),
                        inc="vz", k=1)
                    vz_hist.append(sch.c("vz"))
                    state["u"] += 1

                sch.wait("gpsimd", "vz", sch.c("vz"))
                sch.op("gpsimd", lambda e, hg=hg, hp=hp: e.dma_start(
                    rc_d[hg:hg + 1, :], rcp4[hp]), inc="rcs", k=16)
                rcs_hist.append(sch.c("rcs"))
                sch.wait("gpsimd", "rcs", sch.c("rcs"))
                sch.wait("gpsimd", "vr2", vr2_hist[hg])
                sch.op("gpsimd", lambda e, hg=hg, hp=hp: e.dma_start(
                    recH_t[hp], rc_d[hg:hg + 1, :].rearrange(
                        "a (p e) -> (a p) e", p=128)), inc="bc1", k=16)
                sch.wait("vector", "bc1", sch.c("bc1"))
                sch.wait("vector", "rcs2", rcs2_hist[hg])
                sch.op("vector", lambda e, hp=hp: e.reciprocal(
                    recH_r[hp], recH_t[hp]), inc="vr2", k=1)
                vr2_hist.append(sch.c("vr2"))
                sch.wait("gpsimd", "vr2", sch.c("vr2"))
                sch.op("gpsimd", lambda e, hg=hg, hp=hp: e.dma_start(
                    rc2_d[hg:hg + 1, :].rearrange("a (p e) -> (a p) e", p=128),
                    recH_r[hp]), inc="rcs2", k=16)
                rcs2_hist.append(sch.c("rcs2"))
                sch.wait("gpsimd", "rcs2", sch.c("rcs2"))
                sch.wait("gpsimd", "vmul", vmul_hist[hg])
                sch.op("gpsimd", lambda e, hg=hg, hp=hp: e.dma_start(
                    bcbH[hp], rc2_d[hg:hg + 1, :].partition_broadcast(64)),
                    inc="bcs", k=16)
                sch.wait("vector", "bcs", sch.c("bcs"))
                for qb in range(NQBA):
                    mi = state["mi"]
                    sch.wait("vector", "zsp", zsp_hist[mi])
                    sch.op("vector", lambda e, hp=hp, qb=qb, mi=mi: e.tensor_mul(
                        zstg[mi % 2], zun[hp][:, qb * QBA:(qb + 1) * QBA],
                        bcbH[hp][:, qb * QBA:(qb + 1) * QBA]), inc="vmul", k=1)
                    sch.wait("sync", "vmul", sch.c("vmul"))
                    sch.op("sync", lambda e, hg=hg, qb=qb, mi=mi: e.dma_start(
                        zt_d[hg * 64:hg * 64 + 64, qb * QBA:(qb + 1) * QBA],
                        zstg[mi % 2]), inc="zsp", k=16)
                    zsp_hist.append(sch.c("zsp"))
                    state["mi"] += 1
                vmul_hist.append(sch.c("vmul"))

        if ph >= 2:
            wslice_loads(0)
            wslice_loads(1)
            qkv0 = qkv_group_maker(0)
            for i in range(48):
                qkv0(i % 2)
            for pair in range(NPAIR):
                if pair + 2 < NPAIR:
                    wslice_loads(pair + 2)
                if pair == NPAIR - 1:
                    sch.wait("sync", "tq", 48 * NPAIR)
                    sch.wait("sync", "cast", CAST["g"])
                    for dc in range(8):
                        sch.op("sync", lambda e, dc=dc: e.dma_start(
                            wgT[:, dc * 3072:(dc + 1) * 3072],
                            wgb_d[:, dc * 128:(dc + 1) * 128], transpose=True),
                            inc="wgt", k=16)
                attention(pair)
            sch.wait("sync", "cast", CAST["o"])
            for fc in range(8):
                sch.op("sync", lambda e, fc=fc: e.dma_start(
                    wot[:, fc * 1024:(fc + 1) * 1024],
                    wob_d[:, fc * 128:(fc + 1) * 128], transpose=True),
                    inc="wot", k=16)


        if ph >= 3:
            p3 = []
            for i in range(32):
                src, dst = at3[i % 2], xn2[i % 2]

                def in_ready(eng, i=i):
                    sch.wait(eng, "va", 2 * (i + 1))

                def war(eng, i=i):
                    sch.wait(eng, "x2st", 16 * max(0, i - 1))

                p3.append(ln_tile(i, src, dst, O_COL3, junk3,
                                  ("s2sum", "v2stat", "s2istd", "v2aff"),
                                  in_ready, war, p3[i - 2][4] if i >= 2 else None))

            tw_hist = []
            va_hist = []
            for qtl in range(32):
                pp = qtl % 2
                ps = PS[pp]
                if qtl == 0:
                    sch.wait("sync", "zsp", sch.c("zsp"))
                if qtl >= 2:
                    sch.wait("sync", "tw", tw_hist[qtl - 2])
                sch.op("sync", lambda e, qtl=qtl: e.dma_start(
                    zld[qtl % 2].rearrange("p (f q) -> p f q", f=8),
                    zt_d.rearrange("(f p) q -> p f q", p=128)[
                        :, :, qtl * 128:(qtl + 1) * 128]), inc="zl", k=16)
                if qtl == 0:
                    sch.wait("tensor", "es", sch.c("es"))
                    sch.wait("tensor", "vz", sch.c("vz"))
                    sch.wait("tensor", "wot", 16 * 8)
                if qtl >= 2:
                    sch.wait("tensor", "va", va_hist[qtl - 2])
                sch.wait("tensor", "zl", 16 * (qtl + 1))
                for db in range(2):
                    for fc in range(8):
                        sch.op("tensor", lambda e, db=db, fc=fc, ps=ps, qtl=qtl:
                               e.matmul(ps[:, db * 512:(db + 1) * 512],
                                        zld[qtl % 2][:, fc * 128:(fc + 1) * 128],
                                        wot[:, fc * 1024 + db * 512:
                                            fc * 1024 + (db + 1) * 512],
                                        start=(fc == 0), stop=(fc == 7)),
                               inc="tw" if (db == 1 and fc == 7) else None, k=1)
                tw_hist.append(sch.c("tw"))
                if qtl >= 2:
                    sch.wait("sync", "va", va_hist[qtl - 2])
                sch.op("sync", lambda e, qtl=qtl: e.dma_start(
                    x3[qtl % 2], x_d[qtl * 128:(qtl + 1) * 128, :]), inc="x2", k=16)
                sch.wait("vector", "tw", tw_hist[qtl])
                sch.wait("vector", "x2", 16 * (qtl + 1))
                if qtl >= 2:
                    sch.wait("vector", "v2aff", qtl - 1)
                    sch.wait("vector", "asp", 16 * (qtl - 1))
                for db in range(2):
                    sch.op("vector", lambda e, db=db, ps=ps, qtl=qtl: e.tensor_add(
                        at3[qtl % 2][:, db * 512:(db + 1) * 512],
                        ps[:, db * 512:(db + 1) * 512],
                        x3[qtl % 2][:, db * 512:(db + 1) * 512]),
                        inc="va", k=1)
                va_hist.append(sch.c("va"))
                sch.wait("sync", "va", va_hist[qtl])
                sch.op("sync", lambda e, qtl=qtl: e.dma_start(
                    af_d[qtl * 128:(qtl + 1) * 128, :], at3[qtl % 2]),
                    inc="asp", k=16)
                p3[qtl][0]()
                p3[qtl][1]()
                if qtl >= 1:
                    p3[qtl - 1][2]()
                    p3[qtl - 1][3]()
                    sch.wait("sync", "v2aff", qtl)
                    sch.op("sync", lambda e, j=qtl - 1: e.dma_start(
                        x2_d[128 * j:128 * (j + 1), :], xn2[j % 2]),
                        inc="x2st", k=16)
            p3[31][2]()
            p3[31][3]()
            sch.wait("sync", "v2aff", 32)
            sch.op("sync", lambda e: e.dma_start(x2_d[128 * 31:128 * 32, :],
                                                 xn2[1]), inc="x2st", k=16)

        if ph >= 4:
            sch.wait("sync", "tw", 32)
            sch.wait("sync", "cast", CAST["u"])
            for dc in range(8):
                sch.op("sync", lambda e, dc=dc: e.dma_start(
                    wuT[:, dc * 3072:(dc + 1) * 3072],
                    wub_d[:, dc * 128:(dc + 1) * 128], transpose=True),
                    inc="wut", k=16)
            sch.wait("sync", "cast", CAST["d"])
            sch.wait("sync", "x2st", 16 * 32)
            sch.wait("sync", "asp", 16 * 32)
            for hc in range(24):
                sch.op("sync", lambda e, hc=hc: e.dma_start(
                    wdT[:, hc * 1024:(hc + 1) * 1024],
                    wdb_d[:, hc * 128:(hc + 1) * 128], transpose=True),
                    inc="wdt2", k=16)

            tg_hist = {}
            tu_hist = {}
            vact_hist = {}
            vo_hist = {}
            ost_hist = {}
            for b in range(NBF):
                q0 = QBF * b
                bp = b % 2
                sch.wait("sync", "x2st", 16 * 32)
                if b >= 2:
                    sch.wait("sync", "tu", tu_hist[(b - 2, 23)])
                for dc in range(8):
                    sch.op("sync", lambda e, dc=dc, bp=bp, q0=q0: e.dma_start(
                        x2t4[bp][:, dc * 256:(dc + 1) * 256],
                        x2_d[q0:q0 + QBF, dc * 128:(dc + 1) * 128], transpose=True),
                        inc="x2t", k=16)
                for qt2 in range(2):
                    if b >= 1:
                        sch.wait("sync", "vo", vo_hist[(b - 1, qt2)])
                    sch.op("sync", lambda e, qt2=qt2, q0=q0: e.dma_start(
                        res4[qt2], af_d[q0 + qt2 * 128:q0 + (qt2 + 1) * 128, :]),
                        inc="rl", k=16)
                for ht in range(24):
                    p = ht % 2
                    ps = PS[p]
                    if ht >= 2:
                        sch.wait("tensor", "vact", vact_hist[(b, ht - 2)])
                    elif b >= 1:
                        sch.wait("tensor", "vact", vact_hist[(b - 1, 22 + ht)])
                    if ht == 0:
                        sch.wait("tensor", "x2t", 16 * 8 * (b + 1))
                        if b == 0:
                            sch.wait("tensor", "wgt", 16 * 8)
                            sch.wait("tensor", "va", 64)
                    for dc in range(8):
                        sch.op("tensor", lambda e, dc=dc, ht=ht, ps=ps, bp=bp:
                               e.matmul(ps[:, 0:256],
                                        wgT[:, dc * 3072 + ht * 128:
                                            dc * 3072 + (ht + 1) * 128],
                                        x2t4[bp][:, dc * 256:(dc + 1) * 256],
                                        start=(dc == 0), stop=(dc == 7)),
                               inc="tg" if dc == 7 else None, k=1)
                    tg_hist[(b, ht)] = sch.c("tg")
                    if b == 0 and ht == 0:
                        sch.wait("tensor", "wut", 16 * 8)
                    for dc in range(8):
                        sch.op("tensor", lambda e, dc=dc, ht=ht, ps=ps, bp=bp:
                               e.matmul(ps[:, 512:768],
                                        wuT[:, dc * 3072 + ht * 128:
                                            dc * 3072 + (ht + 1) * 128],
                                        x2t4[bp][:, dc * 256:(dc + 1) * 256],
                                        start=(dc == 0), stop=(dc == 7)),
                               inc="tu" if dc == 7 else None, k=1)
                    tu_hist[(b, ht)] = sch.c("tu")
                    sch.wait("scalar", "tg", tg_hist[(b, ht)])
                    if ht >= 2:
                        sch.wait("scalar", "vact", vact_hist[(b, ht - 2)])
                    elif b >= 1:
                        sch.wait("scalar", "vact", vact_hist[(b - 1, 22 + ht)])
                    sch.op("scalar", lambda e, ht=ht, ps=ps: e.activation(
                        silu[ht % 2], ps[:, 0:256], AF.Silu, scale=S127),
                        inc="ss", k=1)
                    sch.wait("vector", "ss", sch.c("ss"))
                    sch.wait("vector", "tu", tu_hist[(b, ht)])
                    if b >= 1 and ht < 2:
                        sch.wait("vector", "td", 4 * b)
                    sch.op("vector", lambda e, ht=ht, ps=ps: e.tensor_mul(
                        actT[:, ht * 256:(ht + 1) * 256], ps[:, 512:768],
                        silu[ht % 2]), inc="vact", k=1)
                    vact_hist[(b, ht)] = sch.c("vact")
                for g, (qt2, db) in enumerate(((0, 0), (0, 1), (1, 0), (1, 1))):
                    psd = PS[2 + g % 2]
                    sch.wait("tensor", "vact", vact_hist[(b, 23)])
                    if b == 0 and g == 0:
                        sch.wait("tensor", "wdt2", 16 * 24)
                    if g >= 2:
                        sch.wait("tensor", "vo", vo_hist[(b, (g - 2) // 2)] - 1)
                    elif b >= 1:
                        sch.wait("tensor", "vo", vo_hist[(b - 1, 1)])
                    for hc in range(24):
                        sch.op("tensor", lambda e, hc=hc, qt2=qt2, db=db, psd=psd:
                               e.matmul(psd[:, 0:512],
                                        actT[:, hc * 256 + qt2 * 128:
                                             hc * 256 + (qt2 + 1) * 128],
                                        wdT[:, hc * 1024 + db * 512:
                                            hc * 1024 + (db + 1) * 512],
                                        start=(hc == 0), stop=(hc == 23)),
                               inc="td" if hc == 23 else None, k=1)
                    sch.wait("vector", "td", sch.c("td"))
                    sch.wait("vector", "rl", 16 * (2 * b + qt2 + 1))
                    if b >= 1:
                        sch.wait("vector", "ost", ost_hist[(b - 1, qt2)])
                    sch.op("vector", lambda e, qt2=qt2, db=db, psd=psd:
                           e.scalar_tensor_tensor(
                               out4[qt2][:, db * 512:(db + 1) * 512],
                               psd[:, 0:512], S127 * S127,
                               res4[qt2][:, db * 512:(db + 1) * 512],
                               ALU.mult, ALU.add), inc="vo", k=1)
                    vo_hist[(b, qt2)] = sch.c("vo")
                    if db == 1:
                        sch.wait("sync", "vo", vo_hist[(b, qt2)])
                        sch.op("sync", lambda e, qt2=qt2, q0=q0: e.dma_start(
                            out_d[q0 + qt2 * 128:q0 + (qt2 + 1) * 128, :],
                            out4[qt2]), inc="ost", k=16)
                        ost_hist[(b, qt2)] = sch.c("ost")

        if ph >= 4:
            sch.wait("sync", "ost", 16 * 32)
        elif ph == 3:
            sch.wait("sync", "asp", 16 * 32)
            sch.wait("sync", "x2st", 16 * 32)
        elif ph == 2:
            sch.wait("sync", "zsp", sch.c("zsp"))
        else:
            sch.wait("sync", "xnst", 16 * 32)

        with nc.Block() as block:
            @block.sync
            def _(eng):
                for f in sch.ops["sync"]:
                    f(eng)

            @block.scalar
            def _(eng):
                for f in sch.ops["scalar"]:
                    f(eng)

            @block.vector
            def _(eng):
                for f in sch.ops["vector"]:
                    f(eng)

            @block.tensor
            def _(eng):
                for f in sch.ops["tensor"]:
                    f(eng)

            @block.gpsimd
            def _(eng):
                for f in sch.ops["gpsimd"]:
                    f(eng)

        for g in reversed(sch._stack):
            g.__exit__(None, None, None)
    return nc


def kernel(L0_attn_x, L0_attn_W_K, L0_attn_W_Q, L0_attn_W_V, L0_attn_W_O,
           L0_ffn_w_gate, L0_ffn_w_up, L0_ffn_w_down):
    x = np.ascontiguousarray(np.asarray(L0_attn_x, dtype=np.float32))
    wq = np.ascontiguousarray(np.asarray(L0_attn_W_Q, np.float32).reshape(D, D))
    wk = np.ascontiguousarray(np.asarray(L0_attn_W_K, np.float32).reshape(D, D))
    wv = np.ascontiguousarray(np.asarray(L0_attn_W_V, np.float32).reshape(D, D))
    wo = np.ascontiguousarray(np.asarray(L0_attn_W_O, np.float32))
    wg = np.ascontiguousarray(np.asarray(L0_ffn_w_gate, np.int32))
    wu = np.ascontiguousarray(np.asarray(L0_ffn_w_up, np.int32))
    wd = np.ascontiguousarray(np.asarray(L0_ffn_w_down, np.int32))
    tm = ((1.0 - np.triu(np.ones((128, 128), np.float32))) * MASKV).astype(
        ml_dtypes.bfloat16)
    idn = np.eye(128, dtype=np.float32).astype(ml_dtypes.bfloat16)
    nc = build()
    in_maps = [{"x": x[c], "wq": wq, "wk": wk, "wv": wv, "wo": wo,
                "wg": wg, "wu": wu, "wd": wd, "trimask": tm, "idn": idn}
               for c in range(8)]
    res = run_bass_kernel_spmd(nc, in_maps, core_ids=list(range(8)))
    return np.stack([np.asarray(res.results[c]["out"], np.float32)
                     for c in range(8)])


# version 07
# speedup vs baseline: 1.0151x; 1.0151x over previous
import numpy as np
import ml_dtypes
import concourse.bass as bass
import concourse.mybir as mybir
from concourse.bass_utils import run_bass_kernel_spmd

F32 = mybir.dt.float32
BF16 = mybir.dt.bfloat16
I32 = mybir.dt.int32
U8 = mybir.dt.uint8
AF = mybir.ActivationFunctionType
ALU = mybir.AluOpType

S, D, H, DH, HID = 4096, 1024, 16, 64, 3072
NPAIR = 8
QBA = 1024
NQBA = S // QBA
QBF = 256
NBF = S // QBF
S127 = 1.0 / 127.0
SM_SCALE = 0.125
CBIAS = -3.0
MASKV = -960.0
EPS = 1e-5
ARENA = 210944

O_XNT = 0
O_QT = 65536
O_KT = 81920
O_V = 98304
O_WSL = 114944
O_EXPS = 127232
O_BCB = 131328
O_RCPZ = 135424
O_IDN = 139520
O_CB = 139776
O_REC = 139840
O_TRIM = 140032
O_EPSC = 140288
O_EPS3 = 192512
O_X1 = 141312
O_XN1 = 157696
O_JUNK = 165888
O_COL1 = 167936
O_WGT = 0
O_X3 = 147456
O_AT3 = 155648
O_XN2 = 172032
O_ZLD = 180224
O_COL3 = 188416
O_JUNK3 = 190464
O_WOT = 192576
O_WUT = 49152
O_WDT = 98304
O_X2T4 = 147456
O_ACTT = 155648
O_SILU = 167936
O_RES4 = 169984
O_OUT4 = 178176


class Sched:

    def __init__(self, nc):
        self.nc = nc
        self.ops = {e: [] for e in ("sync", "scalar", "vector", "tensor", "gpsimd")}
        self.sems = {}
        self.cnt = {}
        self._stack = []

    def sem(self, name):
        g = self.nc.semaphore(name)
        h = g.__enter__()
        self._stack.append(g)
        self.sems[name] = h
        self.cnt[name] = 0

    def wait(self, eng, name, val=None):
        v = self.cnt[name] if val is None else val
        if v <= 0:
            return
        h = self.sems[name]
        self.ops[eng].append(lambda e, h=h, v=v: e.wait_ge(h, v))

    def op(self, eng, fn, inc=None, k=1):
        if inc is None:
            self.ops[eng].append(fn)
        else:
            h = self.sems[inc]
            self.cnt[inc] += k
            self.ops[eng].append(lambda e, fn=fn, h=h, k=k: fn(e).then_inc(h, k))

    def c(self, name):
        return self.cnt[name]


def build(dbg=False, ph=4):
    nc = bass.Bass("TRN2", target_bir_lowering=False)
    x_d = nc.dram_tensor("x", [S, D], F32, kind="ExternalInput").ap()
    wq_d = nc.dram_tensor("wq", [D, D], F32, kind="ExternalInput").ap()
    wk_d = nc.dram_tensor("wk", [D, D], F32, kind="ExternalInput").ap()
    wv_d = nc.dram_tensor("wv", [D, D], F32, kind="ExternalInput").ap()
    wo_d = nc.dram_tensor("wo", [D, D], F32, kind="ExternalInput").ap()
    wg_d = nc.dram_tensor("wg", [HID, D], I32, kind="ExternalInput").ap()
    wu_d = nc.dram_tensor("wu", [HID, D], I32, kind="ExternalInput").ap()
    wd_d = nc.dram_tensor("wd", [D, HID], I32, kind="ExternalInput").ap()
    tm_d = nc.dram_tensor("trimask", [128, 128], BF16, kind="ExternalInput").ap()
    id_d = nc.dram_tensor("idn", [128, 128], BF16, kind="ExternalInput").ap()
    out_d = nc.dram_tensor("out", [S, D], F32, kind="ExternalOutput").ap()
    ik = "ExternalOutput" if dbg else "Internal"
    xn_d = nc.dram_tensor("xn_bf", [S, D], BF16, kind=ik).ap()
    af_d = nc.dram_tensor("attn_f", [S, D], F32, kind=ik).ap()
    x2_d = nc.dram_tensor("xn2_bf", [S, D], BF16, kind=ik).ap()
    zt_d = nc.dram_tensor("zt_bf", [D, S], BF16, kind="Internal").ap()
    wqb_d = nc.dram_tensor("wq_bf", [D, D], BF16, kind="Internal").ap()
    wkb_d = nc.dram_tensor("wk_bf", [D, D], BF16, kind="Internal").ap()
    wvb_d = nc.dram_tensor("wv_bf", [D, D], BF16, kind="Internal").ap()
    wob_d = nc.dram_tensor("wo_bf", [D, D], BF16, kind="Internal").ap()
    wgb_d = nc.dram_tensor("wg_bf", [HID, D], BF16, kind="Internal").ap()
    wub_d = nc.dram_tensor("wu_bf", [HID, D], BF16, kind="Internal").ap()
    wdb_d = nc.dram_tensor("wd_bf", [D, HID], BF16, kind="Internal").ap()
    rc_d = nc.dram_tensor("rc", [H, S], F32, kind="Internal").ap()
    rc2_d = nc.dram_tensor("rc2", [H, S], F32, kind="Internal").ap()

    with (
        nc.sbuf_tensor("arena", [128, ARENA], U8) as arena,
        nc.psum_tensor("ps0", [128, 1024], F32) as ps0,
        nc.psum_tensor("ps1", [128, 1024], F32) as ps1,
        nc.psum_tensor("ps2", [128, 1024], F32) as ps2,
        nc.psum_tensor("ps3", [128, 1024], F32) as ps3,
    ):
        PS = [ps0, ps1, ps2, ps3]

        def cv(off, p, n, dt, p0=0):
            nb = {BF16: 2, F32: 4, I32: 4}[dt] * n
            return arena[p0:p0 + p, off:off + nb].bitcast(dt)

        xnT = cv(O_XNT, 128, 32768, BF16)
        qt = [cv(O_QT + par * 8192, 128, 4096, BF16) for par in (0, 1)]
        kt = [cv(O_KT + par * 8192, 128, 4096, BF16) for par in (0, 1)]
        vsb = [cv(O_V + par * 8320, 128, 4160, BF16) for par in (0, 1)]
        wsl = [[cv(O_WSL + (m * 2 + par) * 2048, 128, 1024, BF16) for par in (0, 1)]
               for m in range(3)]
        exps = [cv(O_EXPS, 128, 1024, BF16), cv(O_EXPS + 2048, 128, 1024, BF16),
                cv(174080, 128, 1024, BF16), cv(176128, 128, 1024, BF16)]
        zstg = [cv(O_RCPZ + par * 2048, 64, 1024, BF16, p0=64) for par in (0, 1)]
        zun = [cv(141312 + hp * 8192, 64, 4096, BF16) for hp in (0, 1)]
        bcbH = [cv(157696 + hp * 20480, 64, 4096, F32) for hp in (0, 1)]
        rcp4 = [cv(194560, 1, 4096, F32, p0=32 * hp) for hp in (0, 1)]
        recH_t = [cv(140352 + hp * 128, 128, 32, F32) for hp in (0, 1)]
        recH_r = [cv(140608 + hp * 128, 128, 32, F32) for hp in (0, 1)]
        idn = cv(O_IDN, 128, 128, BF16)
        cbias = cv(O_CB, 128, 1, F32)
        trim = cv(O_TRIM, 128, 128, BF16)
        eps_c = cv(O_EPSC, 128, 1, F32)
        eps_c3 = cv(O_EPS3, 128, 1, F32)
        x1 = [cv(O_X1 + par * 4096, 128, 1024, F32) for par in range(4)]
        xn1 = [cv(O_XN1 + par * 2048, 128, 1024, BF16) for par in range(4)]
        junk = cv(O_JUNK, 128, 1024, BF16)
        wgT = cv(O_WGT, 128, 24576, BF16)
        x3 = [cv(O_X3 + par * 4096, 128, 1024, F32) for par in (0, 1)]
        at3 = [cv(O_AT3 + par * 4096, 128, 1024, F32) for par in range(4)]
        xn2 = [cv(O_XN2 + par * 2048, 128, 1024, BF16) for par in range(4)]
        zld = [cv(O_ZLD + par * 2048, 128, 1024, BF16) for par in range(4)]
        junk3 = cv(O_JUNK3, 128, 1024, BF16)
        wot = cv(O_WOT, 128, 8192, BF16)
        wuT = cv(O_WUT, 128, 24576, BF16)
        wdT = cv(O_WDT, 128, 24576, BF16)
        x2t4 = [cv(O_X2T4 + par * 4096, 128, 2048, BF16) for par in (0, 1)]
        actT = cv(O_ACTT, 128, 6144, BF16)
        silu = [cv(O_SILU + par * 1024, 128, 256, F32) for par in (0, 1)]
        res4 = [cv(O_RES4 + par * 4096, 128, 1024, F32) for par in (0, 1)]
        out4 = [cv(O_OUT4 + par * 4096, 128, 1024, F32) for par in (0, 1)]

        def col(base, idx):
            return cv(base + idx * 64, 128, 1, F32)

        sch = Sched(nc)
        for s in ("vch", "sln", "pre", "cast", "xld", "ssum", "vstat", "sistd",
                  "vaff", "xnst", "xnt", "wsl", "tq", "vq", "ts", "es", "ta",
                  "vr", "rcs", "bc1", "vr2", "rcs2", "bcs", "vz", "zsp", "vmul", "wgt",
                  "wot", "wut", "wdt2", "tw", "x2", "va", "zl", "s2sum", "v2stat",
                  "s2istd", "v2aff", "asp", "x2st", "x2t", "tg", "tu", "ss",
                  "vact", "td", "rl", "vo", "ost"):
            sch.sem(s)

        sch.op("sync", lambda e: e.dma_start(trim, tm_d), inc="pre", k=16)
        sch.op("sync", lambda e: e.dma_start(idn, id_d), inc="pre", k=16)
        sch.op("vector", lambda e: e.memset(cbias, CBIAS), inc="pre", k=1)
        sch.op("vector", lambda e: e.memset(vsb[0][:, 64::65], 1.0), inc="pre", k=1)
        sch.op("vector", lambda e: e.memset(vsb[1][:, 64::65], 1.0), inc="pre", k=1)
        sch.op("vector", lambda e: e.memset(eps_c, EPS), inc="pre", k=1)
        sch.op("vector", lambda e: e.memset(eps_c3, EPS), inc="pre", k=1)
        PRE_N = 37
        sch.wait("vector", "pre", PRE_N)
        for w_src, w_dst in ((wq_d, wqb_d), (wk_d, wkb_d), (wv_d, wvb_d),
                             (wo_d, wob_d), (wg_d, wgb_d), (wu_d, wub_d),
                             (wd_d, wdb_d)):
            sch.op("gpsimd", lambda e, a=w_dst, b=w_src: e.dma_start(a, b),
                   inc="cast", k=16)
        CAST = {"q": 16, "k": 32, "v": 48, "o": 64, "g": 80, "u": 96, "d": 112}

        def ln_batched(cbase, sems, epsv, jk):
            s_sum, s_vst, s_ist, s_aff = sems
            sumA = cv(cbase + 0, 128, 32, F32)
            sqA = cv(cbase + 128, 128, 32, F32)
            mA = cv(cbase + 256, 128, 32, F32)
            msqA = cv(cbase + 384, 128, 32, F32)
            vA = cv(cbase + 512, 128, 32, F32)
            t1A = cv(cbase + 640, 128, 32, F32)
            isA = cv(cbase + 768, 128, 32, F32)
            ngA = cv(cbase + 896, 128, 32, F32)
            base = {"sum": sch.c(s_sum), "ist": sch.c(s_ist)}

            def emit_sums(i, src, in_ready):
                in_ready("scalar")
                sch.op("scalar", lambda e, i=i, src=src: e.activation(
                    jk, src, AF.Copy, accum_out=sumA[:, i:i + 1]),
                    inc=s_sum, k=1)
                sch.op("scalar", lambda e, i=i, src=src: e.activation(
                    jk, src, AF.Square, accum_out=sqA[:, i:i + 1]),
                    inc=s_sum, k=1)

            def emit_stats():
                sch.wait("vector", s_sum, base["sum"] + 64)
                sch.op("vector", lambda e: e.tensor_scalar_mul(mA, sumA, 1.0 / D),
                       inc="vch", k=1)
                sch.wait("vector", "vch", sch.c("vch"))
                sch.op("vector", lambda e: e.tensor_mul(msqA, mA, mA),
                       inc="vch", k=1)
                sch.wait("vector", "vch", sch.c("vch"))
                sch.op("vector", lambda e: e.scalar_tensor_tensor(
                    vA, sqA, 1.0 / D, msqA, ALU.mult, ALU.subtract),
                    inc=s_vst, k=1)
                sch.wait("scalar", s_vst, sch.c(s_vst))
                sch.op("scalar", lambda e: e.activation(t1A, vA, AF.Ln,
                                                        bias=epsv),
                       inc="sln", k=1)
                sch.wait("scalar", "sln", sch.c("sln"))
                sch.op("scalar", lambda e: e.activation(isA, t1A, AF.Exp,
                                                        scale=-0.5),
                       inc=s_ist, k=1)
                sch.wait("vector", s_ist, sch.c(s_ist))
                sch.op("vector", lambda e: e.scalar_tensor_tensor(
                    ngA, mA, -1.0, isA, ALU.mult, ALU.mult), inc=s_ist, k=1)
                base["done"] = sch.c(s_ist)

            def emit_affine(i, src, dst, pre_wait):
                sch.wait("vector", s_ist, base["done"])
                pre_wait("vector")
                sch.op("vector", lambda e, i=i, src=src, dst=dst: e.tensor_scalar(
                    dst, src, isA[:, i:i + 1], ngA[:, i:i + 1], ALU.mult, ALU.add),
                    inc=s_aff, k=1)

            return emit_sums, emit_stats, emit_affine

        ln1_sums, ln1_stats, ln1_affine = ln_batched(
            O_COL1, ("ssum", "vstat", "sistd", "vaff"), eps_c, junk)
        sch.wait("scalar", "pre", PRE_N)
        for i in range(32):
            sch.wait("sync", "ssum", 2 * max(0, i - 3))
            sch.op("sync", lambda e, i=i: e.dma_start(
                x1[i % 4], x_d[128 * i:128 * i + 128, :]), inc="xld", k=16)
            ln1_sums(i, x1[i % 4],
                     lambda eng, i=i: sch.wait(eng, "xld", 16 * (i + 1)))
        ln1_stats()
        for i in range(32):
            sch.wait("sync", "vaff", max(0, i - 3))
            if i < 4:
                sch.wait("sync", "ssum", 64)
            sch.op("sync", lambda e, i=i: e.dma_start(
                x1[i % 4], x_d[128 * i:128 * i + 128, :]), inc="xld", k=16)

            def prew(eng, i=i):
                sch.wait(eng, "xld", 16 * (33 + i))
                sch.wait(eng, "xnst", 16 * max(0, i - 3))
            ln1_affine(i, x1[i % 4], xn1[i % 4], prew)
            sch.wait("sync", "vaff", i + 1)
            sch.op("sync", lambda e, j=i: e.dma_start(
                xn_d[128 * j:128 * j + 128, :], xn1[j % 4]), inc="xnst", k=16)
        for half in range(2):
            sch.wait("sync", "xnst", 16 * (16 if half == 0 else 32))
            for dc in range(8):
                sch.op("sync", lambda e, dc=dc, half=half: e.dma_start(
                    xnT[:, dc * 4096 + half * 2048:dc * 4096 + half * 2048 + 2048],
                    xn_d[half * 2048:half * 2048 + 2048,
                         dc * 128:(dc + 1) * 128],
                    transpose=True), inc="xnt", k=16)

        def wslice_loads(pair):
            par = pair % 2
            sch.wait("sync", "tq", 48 * max(0, pair - 1))
            sch.wait("sync", "cast", CAST["v"])
            for m, wb in enumerate((wqb_d, wkb_d, wvb_d)):
                for dc in range(8):
                    sch.op("sync", lambda e, m=m, par=par, dc=dc, wb=wb, pr=pair:
                           e.dma_start(wsl[m][par][:, dc * 128:(dc + 1) * 128],
                                       wb[pr * 128:(pr + 1) * 128,
                                          dc * 128:(dc + 1) * 128],
                                       transpose=True), inc="wsl", k=16)

        state = {"u": 0, "k": 0, "mi": 0}
        es_hist = []
        ta_hist = []
        vz_hist = [0, 0]
        zsp_hist = [0, 0]
        rcs_hist = [0, 0]
        vr2_hist = [0, 0]
        rcs2_hist = [0, 0]
        vmul_hist = [0, 0]
        vq_by_pair = {}

        borrowed = {0: 0, 1: 0}

        def qkv_group_maker(pair):
            par = pair % 2
            gstate = {"gi": 0, "half": 0}
            vq_hist = []
            vq_by_pair[pair] = vq_hist

            def emit_group(wt, psum_wait=None):
                gi = gstate["gi"]
                ps = PS[2 + wt]
                half = gstate["half"]
                gstate["half"] ^= 1
                psl = slice(0, 512) if half == 0 else slice(512, 1024)
                if gi >= 2:
                    sch.wait("tensor", "vq", vq_hist[gi - 2])
                if psum_wait:
                    psum_wait()
                if pair == 0:
                    if gi == 0:
                        sch.wait("tensor", "pre", PRE_N)
                    lo = ((gi if gi < 8 else gi - 8) < 4) if gi < 16 \
                        else (gi - 16 < 16)
                    sch.wait("tensor", "xnt", 128 if lo else 256)
                sch.wait("tensor", "wsl", 16 * 24 * (pair + 1))
                if gi < 8:
                    qb = gi
                    lhsT = lambda dc, par=par: wsl[0][par][:, dc * 128:(dc + 1) * 128]
                    rhs = lambda dc, qb=qb: xnT[:, dc * 4096 + qb * 512:
                                                dc * 4096 + (qb + 1) * 512]
                    n = 512
                elif gi < 16:
                    qb = gi - 8
                    lhsT = lambda dc, par=par: wsl[1][par][:, dc * 128:(dc + 1) * 128]
                    rhs = lambda dc, qb=qb: xnT[:, dc * 4096 + qb * 512:
                                                dc * 4096 + (qb + 1) * 512]
                    n = 512
                else:
                    pt = gi - 16
                    lhsT = lambda dc, pt=pt: xnT[:, dc * 4096 + pt * 128:
                                                 dc * 4096 + (pt + 1) * 128]
                    rhs = lambda dc, par=par: wsl[2][par][:, dc * 128:(dc + 1) * 128]
                    n = 128
                for dc in range(8):
                    sch.op("tensor", lambda e, dc=dc, ps=ps, l=lhsT, r=rhs, n=n,
                           psl=psl: e.matmul(ps[0:128, psl][:, 0:n], l(dc),
                                             r(dc), start=(dc == 0),
                                             stop=(dc == 7)),
                           inc="tq" if dc == 7 else None, k=1)
                borrowed[wt] = None
                sch.wait("vector", "tq", sch.c("tq"))
                if gi < 8:
                    qb = gi
                    sch.op("vector", lambda e, ps=ps, qb=qb, par=par, psl=psl:
                           e.tensor_copy(qt[par][:, qb * 512:(qb + 1) * 512],
                                         ps[0:128, psl]), inc="vq", k=1)
                elif gi < 16:
                    qb = gi - 8
                    sch.op("vector", lambda e, ps=ps, qb=qb, par=par, psl=psl:
                           e.tensor_copy(kt[par][:, qb * 512:(qb + 1) * 512],
                                         ps[0:128, psl]), inc="vq", k=1)
                else:
                    pt = gi - 16
                    sch.op("vector", lambda e, ps=ps, pt=pt, par=par, psl=psl:
                           e.tensor_copy(vsb[par][:, pt * 65:pt * 65 + 64],
                                         ps[0:128, psl][:, 0:64]), inc="vq", k=1)
                    sch.op("vector", lambda e, ps=ps, pt=pt, par=par, psl=psl:
                           e.tensor_copy(
                               vsb[par][:, 2080 + pt * 65:2080 + pt * 65 + 64],
                               ps[0:128, psl][:, 64:128]), inc="vq", k=1)
                vq_hist.append(sch.c("vq"))
                borrowed[wt] = sch.c("vq")
                gstate["gi"] += 1

            return emit_group

        def attention(pair):
            par = pair % 2
            nxt = qkv_group_maker(pair + 1) if pair + 1 < NPAIR else None
            n_iters = 160
            it = [0]
            gquota = [0]

            def maybe_groups():
                if nxt is None:
                    return
                it[0] += 1
                want = max(0, (it[0] - 33) * 48 // (n_iters - 33))
                while gquota[0] < min(48, want):
                    u = state["u"]
                    wt = (u + 1) % 2

                    def psum_wait(u=u):
                        sch.wait("tensor", "vz", vz_hist[u + 1])
                    nxt(wt, psum_wait)
                    gquota[0] += 1

            for hh in (0, 1):
                hg = 2 * pair + hh
                hp = hg % 2
                fb = 64 * hh
                vo_ = 2080 * hh
                for qb in range(NQBA):
                    q0 = qb * QBA
                    u = state["u"]
                    zq = u % 2
                    psz = PS[2 + zq]
                    npt = 8 * qb + 8
                    pend = []

                    def emit_A(psz=psz, vo_=vo_, npt=npt, u=u, par=par):
                        pt, segs, sp, es_at = pend.pop(0)
                        sch.wait("tensor", "es", es_at)
                        if pt == 0:
                            sch.wait("tensor", "vz", vz_hist[u])
                            sch.wait("tensor", "vq", borrowed[u % 2])
                        for si, (a, b) in enumerate(segs):
                            sch.op("tensor", lambda e, a=a, b=b, psz=psz, vo_=vo_,
                                   pt=pt, npt=npt, sp=sp, par=par: e.matmul(
                                       psz[0:65, a:b],
                                       vsb[par][:, vo_ + pt * 65:vo_ + pt * 65 + 65],
                                       exps[sp][:, a:b],
                                       start=(pt == 0), stop=(pt == npt - 1)),
                                   inc="ta" if si == len(segs) - 1 else None, k=1)
                        ta_hist.append(sch.c("ta"))
                        maybe_groups()

                    for pt in range(npt):
                        k = state["k"]
                        sp = k % 4
                        pss = PS[k % 2]
                        p0 = 128 * pt
                        co = max(0, p0 - q0)
                        segs = [(co, 512), (512, 1024)] if co < 512 else [(co, 1024)]
                        diag = pt >= 8 * qb
                        if k >= 2:
                            sch.wait("tensor", "es", es_hist[k - 2])
                        if pt == 0 and hh == 0 and qb == 0:
                            sch.wait("tensor", "vq", sch.c("vq"))
                        for si, (a, b) in enumerate(segs):
                            first = si == 0
                            last_seg = si == len(segs) - 1
                            dm = diag and a == co
                            sch.op("tensor", lambda e, a=a, b=b, pss=pss, fb=fb,
                                   p0=p0, q0=q0, par=par, dm=dm: e.matmul(
                                       pss[:, a:b],
                                       kt[par][fb:fb + 64, p0:p0 + 128],
                                       qt[par][fb:fb + 64, q0 + a:q0 + b],
                                       start=True, stop=not dm),
                                   inc=None if dm else
                                       ("ts" if last_seg else None), k=1)
                            if dm:
                                sch.op("tensor", lambda e, a=a, pss=pss: e.matmul(
                                    pss[:, a:a + 128], idn, trim,
                                    start=False, stop=True),
                                    inc="ts" if last_seg else None, k=1)
                        sch.wait("scalar", "ts", sch.c("ts"))
                        if k >= 4:
                            sch.wait("scalar", "ta", ta_hist[k - 4])
                        sch.op("scalar", lambda e, sp=sp, co=co, pss=pss:
                               e.activation(exps[sp][:, co:1024], pss[:, co:1024],
                                            AF.Exp, bias=cbias, scale=SM_SCALE),
                               inc="es", k=1)
                        es_hist.append(sch.c("es"))
                        pend.append((pt, segs, sp, sch.c("es")))
                        if len(pend) > 2:
                            emit_A()
                        state["k"] += 1
                    while pend:
                        emit_A()
                    sch.wait("vector", "ta", sch.c("ta"))
                    sch.wait("vector", "vmul", vmul_hist[hg])
                    sch.op("vector", lambda e, hp=hp, q0=q0, psz=psz: e.tensor_copy(
                        zun[hp][:, q0:q0 + QBA], psz[0:64, 0:1024]), inc="vz", k=1)
                    sch.wait("vector", "rcs", rcs_hist[hg])
                    sch.op("vector", lambda e, hp=hp, q0=q0, psz=psz: e.tensor_copy(
                        rcp4[hp][:, q0:q0 + QBA], psz[64:65, 0:1024]),
                        inc="vz", k=1)
                    vz_hist.append(sch.c("vz"))
                    state["u"] += 1

                sch.wait("gpsimd", "vz", sch.c("vz"))
                sch.op("gpsimd", lambda e, hg=hg, hp=hp: e.dma_start(
                    rc_d[hg:hg + 1, :], rcp4[hp]), inc="rcs", k=16)
                rcs_hist.append(sch.c("rcs"))
                sch.wait("gpsimd", "rcs", sch.c("rcs"))
                sch.wait("gpsimd", "vr2", vr2_hist[hg])
                sch.op("gpsimd", lambda e, hg=hg, hp=hp: e.dma_start(
                    recH_t[hp], rc_d[hg:hg + 1, :].rearrange(
                        "a (p e) -> (a p) e", p=128)), inc="bc1", k=16)
                sch.wait("vector", "bc1", sch.c("bc1"))
                sch.wait("vector", "rcs2", rcs2_hist[hg])
                sch.op("vector", lambda e, hp=hp: e.reciprocal(
                    recH_r[hp], recH_t[hp]), inc="vr2", k=1)
                vr2_hist.append(sch.c("vr2"))
                sch.wait("gpsimd", "vr2", sch.c("vr2"))
                sch.op("gpsimd", lambda e, hg=hg, hp=hp: e.dma_start(
                    rc2_d[hg:hg + 1, :].rearrange("a (p e) -> (a p) e", p=128),
                    recH_r[hp]), inc="rcs2", k=16)
                rcs2_hist.append(sch.c("rcs2"))
                sch.wait("gpsimd", "rcs2", sch.c("rcs2"))
                sch.wait("gpsimd", "vmul", vmul_hist[hg])
                sch.op("gpsimd", lambda e, hg=hg, hp=hp: e.dma_start(
                    bcbH[hp], rc2_d[hg:hg + 1, :].partition_broadcast(64)),
                    inc="bcs", k=16)
                sch.wait("vector", "bcs", sch.c("bcs"))
                for qb in range(NQBA):
                    mi = state["mi"]
                    sch.wait("vector", "zsp", zsp_hist[mi])
                    sch.op("vector", lambda e, hp=hp, qb=qb, mi=mi: e.tensor_mul(
                        zstg[mi % 2], zun[hp][:, qb * QBA:(qb + 1) * QBA],
                        bcbH[hp][:, qb * QBA:(qb + 1) * QBA]), inc="vmul", k=1)
                    sch.wait("sync", "vmul", sch.c("vmul"))
                    sch.op("sync", lambda e, hg=hg, qb=qb, mi=mi: e.dma_start(
                        zt_d[hg * 64:hg * 64 + 64, qb * QBA:(qb + 1) * QBA],
                        zstg[mi % 2]), inc="zsp", k=16)
                    zsp_hist.append(sch.c("zsp"))
                    state["mi"] += 1
                vmul_hist.append(sch.c("vmul"))

        if ph >= 2:
            wslice_loads(0)
            wslice_loads(1)
            qkv0 = qkv_group_maker(0)
            for i in range(48):
                qkv0(i % 2)
            for pair in range(NPAIR):
                if pair + 2 < NPAIR:
                    wslice_loads(pair + 2)
                if pair == NPAIR - 1:
                    sch.wait("sync", "tq", 48 * NPAIR)
                    sch.wait("sync", "cast", CAST["g"])
                    for dc in range(8):
                        sch.op("sync", lambda e, dc=dc: e.dma_start(
                            wgT[:, dc * 3072:(dc + 1) * 3072],
                            wgb_d[:, dc * 128:(dc + 1) * 128], transpose=True),
                            inc="wgt", k=16)
                attention(pair)
            sch.wait("sync", "cast", CAST["o"])
            for fc in range(8):
                sch.op("sync", lambda e, fc=fc: e.dma_start(
                    wot[:, fc * 1024:(fc + 1) * 1024],
                    wob_d[:, fc * 128:(fc + 1) * 128], transpose=True),
                    inc="wot", k=16)


        if ph >= 3:
            ln2_sums, ln2_stats, ln2_affine = ln_batched(
                O_COL3, ("s2sum", "v2stat", "s2istd", "v2aff"), eps_c3, junk3)

            tw_hist = []
            va_hist = []
            for qtl in range(32):
                pp = qtl % 2
                ps = PS[pp]
                if qtl == 0:
                    sch.wait("sync", "zsp", sch.c("zsp"))
                if qtl >= 4:
                    sch.wait("sync", "tw", tw_hist[qtl - 4])
                sch.op("sync", lambda e, qtl=qtl: e.dma_start(
                    zld[qtl % 4].rearrange("p (f q) -> p f q", f=8),
                    zt_d.rearrange("(f p) q -> p f q", p=128)[
                        :, :, qtl * 128:(qtl + 1) * 128]), inc="zl", k=16)
                if qtl == 0:
                    sch.wait("tensor", "es", sch.c("es"))
                    sch.wait("tensor", "vz", sch.c("vz"))
                    sch.wait("tensor", "wot", 16 * 8)
                if qtl >= 2:
                    sch.wait("tensor", "va", va_hist[qtl - 2])
                sch.wait("tensor", "zl", 16 * (qtl + 1))
                for db in range(2):
                    for fc in range(8):
                        sch.op("tensor", lambda e, db=db, fc=fc, ps=ps, qtl=qtl:
                               e.matmul(ps[:, db * 512:(db + 1) * 512],
                                        zld[qtl % 4][:, fc * 128:(fc + 1) * 128],
                                        wot[:, fc * 1024 + db * 512:
                                            fc * 1024 + (db + 1) * 512],
                                        start=(fc == 0), stop=(fc == 7)),
                               inc="tw" if (db == 1 and fc == 7) else None, k=1)
                tw_hist.append(sch.c("tw"))
                if qtl >= 2:
                    sch.wait("sync", "va", va_hist[qtl - 2])
                sch.op("sync", lambda e, qtl=qtl: e.dma_start(
                    x3[qtl % 2], x_d[qtl * 128:(qtl + 1) * 128, :]), inc="x2", k=16)
                sch.wait("vector", "tw", tw_hist[qtl])
                sch.wait("vector", "x2", 16 * (qtl + 1))
                if qtl >= 4:
                    sch.wait("vector", "s2sum", 2 * (qtl - 3))
                    sch.wait("vector", "asp", 16 * (qtl - 3))
                for db in range(2):
                    sch.op("vector", lambda e, db=db, ps=ps, qtl=qtl: e.tensor_add(
                        at3[qtl % 4][:, db * 512:(db + 1) * 512],
                        ps[:, db * 512:(db + 1) * 512],
                        x3[qtl % 2][:, db * 512:(db + 1) * 512]),
                        inc="va", k=1)
                va_hist.append(sch.c("va"))
                sch.wait("sync", "va", va_hist[qtl])
                sch.op("sync", lambda e, qtl=qtl: e.dma_start(
                    af_d[qtl * 128:(qtl + 1) * 128, :], at3[qtl % 4]),
                    inc="asp", k=16)
                if qtl < 8:
                    if qtl == 0:
                        sch.wait("sync", "cast", CAST["u"])
                    sch.op("sync", lambda e, dc=qtl: e.dma_start(
                        wuT[:, dc * 3072:(dc + 1) * 3072],
                        wub_d[:, dc * 128:(dc + 1) * 128], transpose=True),
                        inc="wut", k=16)
                else:
                    if qtl == 8:
                        sch.wait("sync", "cast", CAST["d"])
                    sch.op("sync", lambda e, hc=qtl - 8: e.dma_start(
                        wdT[:, hc * 1024:(hc + 1) * 1024],
                        wdb_d[:, hc * 128:(hc + 1) * 128], transpose=True),
                        inc="wdt2", k=16)
                ln2_sums(qtl, at3[qtl % 4],
                         lambda eng, qtl=qtl: sch.wait(eng, "va", 2 * (qtl + 1)))
            ln2_stats()

            def prew3(eng, i):
                sch.wait(eng, "x2", 16 * (33 + i))
                sch.wait(eng, "x2st", 16 * max(0, i - 3))
            for i in range(34):
                if i < 32:
                    sch.wait("sync", "v2aff", max(0, i - 1))
                    if i < 2:
                        sch.wait("sync", "va", 64)
                    sch.op("sync", lambda e, i=i: e.dma_start(
                        x3[i % 2], af_d[128 * i:128 * i + 128, :]), inc="x2", k=16)
                if i >= 2:
                    j = i - 2
                    ln2_affine(j, x3[j % 2], xn2[j % 4],
                               lambda eng, j=j: prew3(eng, j))
                    sch.wait("sync", "v2aff", j + 1)
                    sch.op("sync", lambda e, j=j: e.dma_start(
                        x2_d[128 * j:128 * (j + 1), :], xn2[j % 4]),
                        inc="x2st", k=16)

        if ph >= 4:
            tg_hist = {}
            tu_hist = {}
            vact_hist = {}
            vo_hist = {}
            ost_hist = {}
            for b in range(NBF):
                q0 = QBF * b
                bp = b % 2
                sch.wait("sync", "x2st", 16 * 32)
                if b >= 2:
                    sch.wait("sync", "tu", tu_hist[(b - 2, 23)])
                for dc in range(8):
                    sch.op("sync", lambda e, dc=dc, bp=bp, q0=q0: e.dma_start(
                        x2t4[bp][:, dc * 256:(dc + 1) * 256],
                        x2_d[q0:q0 + QBF, dc * 128:(dc + 1) * 128], transpose=True),
                        inc="x2t", k=16)
                for qt2 in range(2):
                    if b >= 1:
                        sch.wait("sync", "vo", vo_hist[(b - 1, qt2)])
                    sch.op("sync", lambda e, qt2=qt2, q0=q0: e.dma_start(
                        res4[qt2], af_d[q0 + qt2 * 128:q0 + (qt2 + 1) * 128, :]),
                        inc="rl", k=16)
                for ht in range(24):
                    p = ht % 2
                    ps = PS[p]
                    if ht >= 2:
                        sch.wait("tensor", "vact", vact_hist[(b, ht - 2)])
                    elif b >= 1:
                        sch.wait("tensor", "vact", vact_hist[(b - 1, 22 + ht)])
                    if ht == 0:
                        sch.wait("tensor", "x2t", 16 * 8 * (b + 1))
                        if b == 0:
                            sch.wait("tensor", "wgt", 16 * 8)
                            sch.wait("tensor", "va", 64)
                    for dc in range(8):
                        sch.op("tensor", lambda e, dc=dc, ht=ht, ps=ps, bp=bp:
                               e.matmul(ps[:, 0:256],
                                        wgT[:, dc * 3072 + ht * 128:
                                            dc * 3072 + (ht + 1) * 128],
                                        x2t4[bp][:, dc * 256:(dc + 1) * 256],
                                        start=(dc == 0), stop=(dc == 7)),
                               inc="tg" if dc == 7 else None, k=1)
                    tg_hist[(b, ht)] = sch.c("tg")
                    if b == 0 and ht == 0:
                        sch.wait("tensor", "wut", 16 * 8)
                    for dc in range(8):
                        sch.op("tensor", lambda e, dc=dc, ht=ht, ps=ps, bp=bp:
                               e.matmul(ps[:, 512:768],
                                        wuT[:, dc * 3072 + ht * 128:
                                            dc * 3072 + (ht + 1) * 128],
                                        x2t4[bp][:, dc * 256:(dc + 1) * 256],
                                        start=(dc == 0), stop=(dc == 7)),
                               inc="tu" if dc == 7 else None, k=1)
                    tu_hist[(b, ht)] = sch.c("tu")
                    sch.wait("scalar", "tg", tg_hist[(b, ht)])
                    if ht >= 2:
                        sch.wait("scalar", "vact", vact_hist[(b, ht - 2)])
                    elif b >= 1:
                        sch.wait("scalar", "vact", vact_hist[(b - 1, 22 + ht)])
                    sch.op("scalar", lambda e, ht=ht, ps=ps: e.activation(
                        silu[ht % 2], ps[:, 0:256], AF.Silu, scale=S127),
                        inc="ss", k=1)
                    sch.wait("vector", "ss", sch.c("ss"))
                    sch.wait("vector", "tu", tu_hist[(b, ht)])
                    if b >= 1 and ht < 2:
                        sch.wait("vector", "td", 4 * b)
                    sch.op("vector", lambda e, ht=ht, ps=ps: e.tensor_mul(
                        actT[:, ht * 256:(ht + 1) * 256], ps[:, 512:768],
                        silu[ht % 2]), inc="vact", k=1)
                    vact_hist[(b, ht)] = sch.c("vact")
                for g, (qt2, db) in enumerate(((0, 0), (0, 1), (1, 0), (1, 1))):
                    psd = PS[2 + g % 2]
                    sch.wait("tensor", "vact", vact_hist[(b, 23)])
                    if b == 0 and g == 0:
                        sch.wait("tensor", "wdt2", 16 * 24)
                    if g >= 2:
                        sch.wait("tensor", "vo", vo_hist[(b, (g - 2) // 2)] - 1)
                    elif b >= 1:
                        sch.wait("tensor", "vo", vo_hist[(b - 1, 1)])
                    for hc in range(24):
                        sch.op("tensor", lambda e, hc=hc, qt2=qt2, db=db, psd=psd:
                               e.matmul(psd[:, 0:512],
                                        actT[:, hc * 256 + qt2 * 128:
                                             hc * 256 + (qt2 + 1) * 128],
                                        wdT[:, hc * 1024 + db * 512:
                                            hc * 1024 + (db + 1) * 512],
                                        start=(hc == 0), stop=(hc == 23)),
                               inc="td" if hc == 23 else None, k=1)
                    sch.wait("vector", "td", sch.c("td"))
                    sch.wait("vector", "rl", 16 * (2 * b + qt2 + 1))
                    if b >= 1:
                        sch.wait("vector", "ost", ost_hist[(b - 1, qt2)])
                    sch.op("vector", lambda e, qt2=qt2, db=db, psd=psd:
                           e.scalar_tensor_tensor(
                               out4[qt2][:, db * 512:(db + 1) * 512],
                               psd[:, 0:512], S127 * S127,
                               res4[qt2][:, db * 512:(db + 1) * 512],
                               ALU.mult, ALU.add), inc="vo", k=1)
                    vo_hist[(b, qt2)] = sch.c("vo")
                    if db == 1:
                        sch.wait("sync", "vo", vo_hist[(b, qt2)])
                        sch.op("sync", lambda e, qt2=qt2, q0=q0: e.dma_start(
                            out_d[q0 + qt2 * 128:q0 + (qt2 + 1) * 128, :],
                            out4[qt2]), inc="ost", k=16)
                        ost_hist[(b, qt2)] = sch.c("ost")

        if ph >= 4:
            sch.wait("sync", "ost", 16 * 32)
        elif ph == 3:
            sch.wait("sync", "asp", 16 * 32)
            sch.wait("sync", "x2st", 16 * 32)
        elif ph == 2:
            sch.wait("sync", "zsp", sch.c("zsp"))
        else:
            sch.wait("sync", "xnst", 16 * 32)

        with nc.Block() as block:
            @block.sync
            def _(eng):
                for f in sch.ops["sync"]:
                    f(eng)

            @block.scalar
            def _(eng):
                for f in sch.ops["scalar"]:
                    f(eng)

            @block.vector
            def _(eng):
                for f in sch.ops["vector"]:
                    f(eng)

            @block.tensor
            def _(eng):
                for f in sch.ops["tensor"]:
                    f(eng)

            @block.gpsimd
            def _(eng):
                for f in sch.ops["gpsimd"]:
                    f(eng)

        for g in reversed(sch._stack):
            g.__exit__(None, None, None)
    return nc


def kernel(L0_attn_x, L0_attn_W_K, L0_attn_W_Q, L0_attn_W_V, L0_attn_W_O,
           L0_ffn_w_gate, L0_ffn_w_up, L0_ffn_w_down):
    x = np.ascontiguousarray(np.asarray(L0_attn_x, dtype=np.float32))
    wq = np.ascontiguousarray(np.asarray(L0_attn_W_Q, np.float32).reshape(D, D))
    wk = np.ascontiguousarray(np.asarray(L0_attn_W_K, np.float32).reshape(D, D))
    wv = np.ascontiguousarray(np.asarray(L0_attn_W_V, np.float32).reshape(D, D))
    wo = np.ascontiguousarray(np.asarray(L0_attn_W_O, np.float32))
    wg = np.ascontiguousarray(np.asarray(L0_ffn_w_gate, np.int32))
    wu = np.ascontiguousarray(np.asarray(L0_ffn_w_up, np.int32))
    wd = np.ascontiguousarray(np.asarray(L0_ffn_w_down, np.int32))
    tm = ((1.0 - np.triu(np.ones((128, 128), np.float32))) * MASKV).astype(
        ml_dtypes.bfloat16)
    idn = np.eye(128, dtype=np.float32).astype(ml_dtypes.bfloat16)
    nc = build()
    in_maps = [{"x": x[c], "wq": wq, "wk": wk, "wv": wv, "wo": wo,
                "wg": wg, "wu": wu, "wd": wd, "trimask": tm, "idn": idn}
               for c in range(8)]
    res = run_bass_kernel_spmd(nc, in_maps, core_ids=list(range(8)))
    return np.stack([np.asarray(res.results[c]["out"], np.float32)
                     for c in range(8)])
